# version 01
# baseline (speedup 1.0000x reference)
import numpy as np
import ml_dtypes
import concourse.bass as bass
import concourse.mybir as mybir
from concourse.bass_utils import run_bass_kernel_spmd
from contextlib import ExitStack

F32 = mybir.dt.float32
BF16 = mybir.dt.bfloat16
AF = mybir.ActivationFunctionType
ALU = mybir.AluOpType
AX = mybir.AxisListType.X
BF = ml_dtypes.bfloat16

D = 2048
TOK = 2048
TC = 512
NTC = TOK // TC
NDM = D // 128


ENGINES = ("gpsimd", "sync", "tensor", "scalar", "vector")


class Sem:
    __slots__ = ("h", "name", "n")

    def __init__(self, handle, name):
        self.h = handle
        self.name = name
        self.n = 0


class OpRec:
    __slots__ = ("eng", "meth", "args", "kwargs", "incs")

    def __init__(self, eng, meth, args, kwargs):
        self.eng = eng
        self.meth = meth
        self.args = args
        self.kwargs = kwargs
        self.incs = []

    def inc(self, sem: Sem, n: int = 1):
        sem.n += n
        self.incs.append((sem, n, sem.n))
        return self


class EngProxy:
    def __init__(self, sched, eng):
        self._sched = sched
        self._eng = eng

    def wait(self, sem: Sem, val: int | None = None):
        v = sem.n if val is None else val
        self._sched.ops[self._eng].append(("wait", sem, v))

    def __getattr__(self, meth):
        if meth.startswith("_"):
            raise AttributeError(meth)
        sched = self._sched
        eng = self._eng

        def call(*args, **kwargs):
            rec = OpRec(eng, meth, args, kwargs)
            sched.ops[eng].append(("op", rec))
            sched.n_ops += 1
            return rec
        return call


class Sched:

    def __init__(self, nc):
        self.nc = nc
        self.ctx = ExitStack()
        self.ops = {e: [] for e in ENGINES}
        self.n_ops = 0
        self._semcnt = 0
        self.g = EngProxy(self, "gpsimd")
        self.y = EngProxy(self, "sync")
        self.t = EngProxy(self, "tensor")
        self.s = EngProxy(self, "scalar")
        self.v = EngProxy(self, "vector")

    def sem(self, name=None):
        self._semcnt += 1
        h = self.ctx.enter_context(self.nc.semaphore(name or f"sem{self._semcnt}"))
        return Sem(h, name)

    def sbuf(self, name, shape, dtype):
        return self.ctx.enter_context(self.nc.sbuf_tensor(name, list(shape), dtype))

    def psum(self, name, shape, dtype=F32):
        return self.ctx.enter_context(self.nc.psum_tensor(name, list(shape), dtype))

    _CHAIN_ENGINES = ("vector", "scalar", "gpsimd")
    _NOCHAIN_METHS = {"dma_start", "indirect_dma_start", "collective_compute",
                      "nop", "dma_start_transpose"}

    def realize(self):
        nc = self.nc
        chain_sems = {e: self.sem(f"chain_{e}") for e in self._CHAIN_ENGINES}
        block = self.ctx.enter_context(nc.Block())
        engobjs = {
            "gpsimd": nc.gpsimd, "sync": nc.sync, "tensor": nc.tensor,
            "scalar": nc.scalar, "vector": nc.vector,
        }

        def make_body(eng):
            def body(_e):
                obj = engobjs[eng]
                chain = chain_sems.get(eng)
                ccount = 0
                prev_signal = None
                for item in self.ops[eng]:
                    if item[0] == "wait":
                        _, sem, v = item
                        if v > 0:
                            obj.wait_ge(sem.h, v)
                    else:
                        rec = item[1]
                        assert len(rec.incs) <= 1, f"multiple incs on {rec.meth}"
                        do_chain = chain is not None and rec.meth not in self._NOCHAIN_METHS
                        if do_chain and prev_signal is not None:
                            obj.wait_ge(*prev_signal)
                        inst = getattr(obj, rec.meth)(*rec.args, **rec.kwargs)
                        if rec.incs:
                            sem, n, after = rec.incs[0]
                            inst = inst.then_inc(sem.h) if n == 1 else inst.then_inc(sem.h, n)
                            if do_chain:
                                prev_signal = (sem.h, after)
                        elif do_chain:
                            inst = inst.then_inc(chain.h)
                            ccount += 1
                            prev_signal = (chain.h, ccount)
            return body

        for eng in ENGINES:
            if self.ops[eng]:
                getattr(block, eng)(make_body(eng))
        self.ctx.close()
        return nc



def dma_fm(S_engine, dst_tile, src_ap, nchunk, width):
    return S_engine.dma_start(
        out=dst_tile[:].rearrange("p (c n) -> p c n", c=nchunk),
        in_=src_ap.rearrange("(c p) n -> p c n", p=128))

def ts_slice(t, i, w):
    return t[:, i * w:(i + 1) * w]



def host_consts():
    idn = np.eye(128, dtype=BF)
    ones_col = np.ones((128, 1), dtype=BF)
    ones_row = np.ones((1, 128), dtype=BF)
    ones_row_f = np.ones((1, 128), dtype=np.float32)
    sel = np.zeros((8, 8 * 128), dtype=BF)
    for e in range(8):
        sel[e, e * 128:(e + 1) * 128] = 1.0
    return {"c_idn": idn, "c_ones_col": ones_col, "c_ones_row": ones_row,
            "c_ones_row_f": ones_row_f, "c_sel": sel}


def declare_consts(nc):
    return {
        "idn": nc.dram_tensor("c_idn", [128, 128], BF16, kind="ExternalInput").ap(),
        "ones_col": nc.dram_tensor("c_ones_col", [128, 1], BF16, kind="ExternalInput").ap(),
        "ones_row": nc.dram_tensor("c_ones_row", [1, 128], BF16, kind="ExternalInput").ap(),
        "ones_row_f": nc.dram_tensor("c_ones_row_f", [1, 128], F32, kind="ExternalInput").ap(),
        "sel": nc.dram_tensor("c_sel", [8, 8 * 128], BF16, kind="ExternalInput").ap(),
    }


class Common:

    def __init__(self, S: Sched, consts):
        self.S = S
        self.ones_col = S.sbuf("cc_ones_col", (128, 1), BF16)
        self.ones_row = S.sbuf("cc_ones_row", (1, 128), BF16)
        self.ones_row_f = S.sbuf("cc_ones_row_f", (1, 128), F32)
        self.sel = S.sbuf("cc_sel", (8, 8 * 128), BF16)
        self.idn = S.sbuf("cc_idn", (128, 128), BF16)
        ld = S.sem("cc_ld")
        g = S.g
        g.dma_start(out=self.ones_col[:], in_=consts["ones_col"]).inc(ld, 16)
        g.dma_start(out=self.ones_row[:], in_=consts["ones_row"]).inc(ld, 16)
        g.dma_start(out=self.ones_row_f[:], in_=consts["ones_row_f"]).inc(ld, 16)
        g.dma_start(out=self.sel[:], in_=consts["sel"]).inc(ld, 16)
        g.dma_start(out=self.idn[:], in_=consts["idn"]).inc(ld, 16)
        S.t.wait(ld)
        S.v.wait(ld)
        S.s.wait(ld)

    def sel_e(self, e):
        return self.sel[:, e * 128:(e + 1) * 128]



def emit_ln_stats(S, C, src, xhi, sq, p_s1, p_s2, mu_row, rstd_row,
                  p_mu, p_rstd, mm, ev_v, ev_s, eps=1e-5):
    v, s, t = S.v, S.s, S.t
    v.tensor_copy(xhi[:], src[:]).inc(ev_v)
    s.activation(sq[:], src[:], AF.Square).inc(ev_s)
    t.wait(ev_v)
    t.wait(ev_s)
    for dmc in range(NDM):
        t.matmul(p_s1[:], C.ones_col[:], ts_slice(xhi, dmc, TC),
                 start=(dmc == 0), stop=(dmc == NDM - 1))
    for dmc in range(NDM):
        op = t.matmul(p_s2[:], C.ones_col[:], ts_slice(sq, dmc, TC),
                      start=(dmc == 0), stop=(dmc == NDM - 1))
    op.inc(mm)
    v.wait(mm)
    v.tensor_scalar_mul(mu_row[:], p_s1[:], 1.0 / D).inc(ev_v)
    s.wait(ev_v)
    s.activation(rstd_row[:], mu_row[:], AF.Square).inc(ev_s)
    v.wait(ev_s)
    v.scalar_tensor_tensor(rstd_row[:], p_s2[:], 1.0 / D, rstd_row[:],
                           ALU.mult, ALU.subtract)
    v.tensor_scalar_add(rstd_row[:], rstd_row[:], eps).inc(ev_v)
    s.wait(ev_v)
    s.activation(rstd_row[:], rstd_row[:], AF.Sqrt).inc(ev_s)
    v.wait(ev_s)
    v.reciprocal(rstd_row[:], rstd_row[:]).inc(ev_v)
    t.wait(ev_v)
    t.matmul(p_mu[:], C.ones_row_f[:], mu_row[:], start=True, stop=True)
    t.matmul(p_rstd[:], C.ones_row_f[:], rstd_row[:], start=True, stop=True).inc(mm)



def emit_moe(S, C, xT, yT, rw, gate_w, up_w, down_w, scale=0.5):
    g, y, t, s, v = S.g, S.y, S.t, S.s, S.v

    x_tc = S.sbuf("m_x", (128, NDM * TC), F32)
    xhi = S.sbuf("m_xhi", (128, NDM * TC), BF16)
    sq = S.sbuf("m_sq", (128, NDM * TC), BF16)
    h_tc = S.sbuf("m_h", (128, NDM * TC), BF16)
    rw_sb = S.sbuf("m_rw", (128, NDM * 8), BF16)
    mu_row = S.sbuf("m_mu", (1, TC), F32)
    rstd_row = S.sbuf("m_rstd", (1, TC), F32)
    tmp = S.sbuf("m_tmp", (128, TC), F32)
    exrow = S.sbuf("m_ex", (128, 8), F32)
    prow = S.sbuf("m_p", (128, 8), F32)
    m1c = S.sbuf("m_m1", (128, 1), F32)
    m2c = S.sbuf("m_m2", (128, 1), F32)
    dnc = S.sbuf("m_dn", (128, 1), F32)
    keepr = S.sbuf("m_keep", (128, 8), F32)
    crow = S.sbuf("m_c", (128, 8), BF16)
    cT = S.sbuf("m_cT", (8, TC), BF16)
    ce_bc = S.sbuf("m_cebc", (128, 8 * TC), F32)
    gw = [S.sbuf(f"m_gw{i}", (128, NDM * 128), BF16) for i in range(2)]
    uw = [S.sbuf(f"m_uw{i}", (128, NDM * 128), BF16) for i in range(2)]
    dwg = S.sbuf("m_dwg", (128, 8 * D), BF16)
    sg = S.sbuf("m_sg", (128, TC), BF16)
    su = S.sbuf("m_su", (128, TC), BF16)
    a1 = S.sbuf("m_a1", (128, TC), BF16)
    act = S.sbuf("m_act", (128, 8 * TC), BF16)
    yacc = S.sbuf("m_yacc", (128, NDM * TC), F32)

    p_s1 = S.psum("mp_s1", (1, TC))
    p_s2 = S.psum("mp_s2", (1, TC))
    p_mu = S.psum("mp_mu", (128, TC))
    p_rstd = S.psum("mp_rstd", (128, TC))
    p_ct = S.psum("mp_ct", (8, TC), BF16)
    p_e = S.psum("mp_e", (128, TC))
    p_lg = p_e
    p_cb = p_e
    p_g = S.psum("mp_g", (128, TC))
    p_u = S.psum("mp_u", (128, TC))

    ld = S.sem("m_ld")
    wld_r = S.sem("m_wld_r")
    wldgu = [S.sem("m_wldgu0"), S.sem("m_wldgu1")]
    wld_d = S.sem("m_wld_d")
    mm = S.sem("m_mm")
    ev_v = S.sem("m_ev_v")
    ev_s = S.sem("m_ev_s")
    st = S.sem("m_st")

    g.dma_start(out=rw_sb[:], in_=rw[:]).inc(wld_r, 16)

    pair_idx = 0
    pair_evv = []
    dexp_evv = 0
    for tc in range(NTC):
        y.wait(st)
        y.wait(ev_v)
        y.dma_start(out=x_tc[:], in_=xT[tc]).inc(ld, 16)
        v.wait(ld)
        s.wait(ld)

        emit_ln_stats(S, C, x_tc, xhi, sq, p_s1, p_s2, mu_row, rstd_row,
                      p_mu, p_rstd, mm, ev_v, ev_s)

        v.wait(mm)
        for dmc in range(NDM):
            v.tensor_sub(tmp[:], ts_slice(x_tc, dmc, TC), p_mu[:])
            op = v.tensor_mul(ts_slice(h_tc, dmc, TC), tmp[:], p_rstd[:])
        op.inc(ev_v)

        t.wait(ev_v)
        t.wait(wld_r, 16)
        for tt in range(TC // 128):
            for dmc in range(NDM):
                op = t.matmul(p_lg[:, 0:8],
                              h_tc[:, dmc * TC + tt * 128: dmc * TC + tt * 128 + 128],
                              ts_slice(rw_sb, dmc, 8),
                              start=(dmc == 0), stop=(dmc == NDM - 1))
            op.inc(mm)
            v.wait(mm)
            s.wait(mm)
            v.reduce_max(m1c[:], p_lg[:, 0:8], axis=AX, negate=True).inc(ev_v)
            s.wait(ev_v)
            s.activation(exrow[:], p_lg[:, 0:8], AF.Exp, bias=m1c[:]).inc(ev_s)
            v.wait(ev_s)
            v.reduce_sum(m2c[:], exrow[:], axis=AX)
            v.reciprocal(m2c[:], m2c[:])
            v.tensor_scalar_mul(prow[:], exrow[:], m2c[:])
            v.reduce_max(m1c[:], prow[:], axis=AX)
            v.tensor_scalar(keepr[:], prow[:], m1c[:], None, ALU.is_ge)
            v.scalar_tensor_tensor(exrow[:], keepr[:], -1e9, prow[:], ALU.mult, ALU.add)
            v.reduce_max(m2c[:], exrow[:], axis=AX)
            v.tensor_scalar(keepr[:], prow[:], m2c[:], None, ALU.is_ge)
            v.tensor_add(dnc[:], m1c[:], m2c[:])
            v.reciprocal(dnc[:], dnc[:])
            v.tensor_mul(prow[:], prow[:], keepr[:])
            v.tensor_scalar(crow[:], prow[:], dnc[:], float(scale),
                            ALU.mult, ALU.mult).inc(ev_v)
            t.wait(ev_v)
            t.wait(ev_s)
            t.transpose(p_ct[:, tt * 128:(tt + 1) * 128], crow[:], C.idn[:]).inc(mm)
        v.wait(mm)
        v.tensor_copy(cT[:], p_ct[:]).inc(ev_v)
        t.wait(ev_v)
        for e in range(8):
            t.matmul(p_cb[:], C.sel_e(e), cT[:], start=True, stop=True).inc(mm)
            v.wait(mm)
            v.tensor_copy(ts_slice(ce_bc, e, TC), p_cb[:]).inc(ev_v)
            t.wait(ev_v)

        for e in range(8):
            g.wait(ev_v, dexp_evv)
            for fj in range(8):
                g.dma_start(out=ts_slice(dwg, fj, D),
                            in_=down_w[e, fj * 128:(fj + 1) * 128, :]).inc(wld_d, 16)
            dw_ready = wld_d.n
            for fj in range(8):
                buf = pair_idx % 2
                if pair_idx >= 2:
                    g.wait(ev_v, pair_evv[pair_idx - 2])
                g.dma_start(out=gw[buf][:], in_=gate_w[e, fj]).inc(wldgu[buf], 16)
                g.dma_start(out=uw[buf][:], in_=up_w[e, fj]).inc(wldgu[buf], 16)
                guready = wldgu[buf].n
                t.wait(wldgu[buf], guready)
                t.wait(ev_s)
                t.wait(ev_v)
                for dmc in range(NDM):
                    t.matmul(p_g[:], ts_slice(gw[buf], dmc, 128),
                             ts_slice(h_tc, dmc, TC),
                             start=(dmc == 0), stop=(dmc == NDM - 1))
                for dmc in range(NDM):
                    op = t.matmul(p_u[:], ts_slice(uw[buf], dmc, 128),
                                  ts_slice(h_tc, dmc, TC),
                                  start=(dmc == 0), stop=(dmc == NDM - 1))
                op.inc(mm)
                s.wait(mm)
                v.wait(mm)
                s.activation(sg[:], p_g[:], AF.Silu).inc(ev_s)
                v.tensor_copy(su[:], p_u[:])
                v.wait(ev_s)
                v.tensor_mul(a1[:], sg[:], su[:])
                v.tensor_mul(ts_slice(act, fj, TC), a1[:], ts_slice(ce_bc, e, TC)) \
                    .inc(ev_v)
                pair_evv.append(ev_v.n)
                pair_idx += 1
            t.wait(wld_d, dw_ready)
            t.wait(ev_v)
            for dmo in range(NDM):
                for fj in range(8):
                    op = t.matmul(p_cb[:],
                                  dwg[:, fj * D + dmo * 128: fj * D + dmo * 128 + 128],
                                  ts_slice(act, fj, TC),
                                  start=(fj == 0), stop=(fj == 7))
                op.inc(mm)
                v.wait(mm)
                if e == 0:
                    v.tensor_copy(ts_slice(yacc, dmo, TC), p_cb[:]).inc(ev_v)
                else:
                    v.tensor_add(ts_slice(yacc, dmo, TC), ts_slice(yacc, dmo, TC),
                                 p_cb[:]).inc(ev_v)
                t.wait(ev_v)
            dexp_evv = ev_v.n

        for dmo in range(NDM):
            op = v.tensor_add(ts_slice(yacc, dmo, TC), ts_slice(x_tc, dmo, TC),
                              ts_slice(yacc, dmo, TC))
        op.inc(ev_v)
        y.wait(ev_v)
        y.dma_start(out=yT[tc], in_=yacc[:]).inc(st, 16)
    y.wait(st)



def emit_griffin(S, C, xT, yT, wa_w, wi_w, wv_w, wout_w, wgu_w, wdn_w,
                 wcols, masks, s_dram, d_dram, ag_in, ag_out):
    g, y, t, s, v = S.g, S.y, S.t, S.s, S.v
    CH = 64

    x_tc = S.sbuf("a_x", (128, NDM * TC), F32)
    zx = S.sbuf("a_zx", (128, NDM * TC), F32)
    hbuf = S.sbuf("a_h", (128, NDM * TC), BF16)
    xhi = S.sbuf("a_xhi", (128, NDM * TC), BF16)
    sq = hbuf
    scr = S.sbuf("a_scr", (128, 8 * D), BF16)
    wgu = [S.sbuf(f"a_wgu{i}", (128, 2 * D), BF16) for i in range(2)]
    wo = [S.sbuf(f"a_wo{i}", (128, D), BF16) for i in range(2)]
    sdl = [S.sbuf(f"a_sdl{i}", (128, 2 * TC), BF16) for i in range(2)]
    af = S.sbuf("a_af", (128, TC), F32)
    if_ = S.sbuf("a_if", (128, TC), F32)
    vf = S.sbuf("a_vf", (128, TC), F32)
    t1f = S.sbuf("a_t1", (128, TC), F32)
    cdt = S.sbuf("a_cd", (128, TC), F32)
    srf = S.sbuf("a_sr", (128, TC), F32)
    drf = S.sbuf("a_dr", (128, TC), F32)
    onesf = S.sbuf("a_ones", (128, TC), F32)
    carry = S.sbuf("a_carry", (128, 16), F32)
    dcarry = S.sbuf("a_dcarry", (128, 16), F32)
    st0 = S.sbuf("a_st0", (128, 16), F32)
    t3 = S.sbuf("a_t3", (128, 16), F32)
    agsb = S.sbuf("a_agsb", (128, 32), F32)
    rtile = [S.sbuf(f"a_r{i}", (128, 32), F32) for i in range(4)]
    wc_sb = S.sbuf("a_wc", (128, 48), F32)
    mk_sb = S.sbuf("a_mk", (128, 8), F32)
    lnmu = S.sbuf("a_lnmu", (1, TOK), F32)
    lnrstd = S.sbuf("a_lnrstd", (1, TOK), F32)
    mu_row = S.sbuf("a_mu", (1, TC), F32)
    rstd_row = S.sbuf("a_rstd", (1, TC), F32)
    rr_row = S.sbuf("a_rr", (1, TC), F32)
    t2_row = S.sbuf("a_t2", (1, TC), F32)

    p_s1 = S.psum("ap_s1", (1, TC))
    p_s2 = S.psum("ap_s2", (1, TC))
    p_mu = S.psum("ap_mu", (128, TC))
    p_rstd = S.psum("ap_rstd", (128, TC))
    pA = S.psum("ap_A", (128, TC))
    pB = S.psum("ap_B", (128, TC))
    pC = S.psum("ap_C", (128, TC))

    ld = S.sem("a_ld")
    cld = S.sem("a_cld")
    mm = S.sem("a_mm")
    ev_v = S.sem("a_ev_v")
    ev_s = S.sem("a_ev_s")
    st = S.sem("a_st")
    wl = [S.sem(f"a_wl{i}") for i in range(2)]
    wol = [S.sem(f"a_wol{i}") for i in range(2)]
    sdls = [S.sem(f"a_sdls{i}") for i in range(2)]
    sdst = [S.sem(f"a_sdst{i}") for i in range(2)]
    dnl = S.sem("a_dnl")
    agst = S.sem("a_agst")
    agc = S.sem("a_agc")
    agld = S.sem("a_agld")

    g.dma_start(out=wc_sb[:], in_=wcols).inc(cld, 16)
    g.dma_start(out=mk_sb[:], in_=masks).inc(cld, 16)
    v.wait(cld, 32)
    s.wait(cld, 32)
    t.wait(cld, 32)
    v.memset(onesf[:], 1.0)
    v.memset(carry[:], 0.0)
    v.memset(dcarry[:], 1.0).inc(ev_v)
    wpre_c = lambda k: wc_sb[:, k:k + 1]
    wffn_c = lambda k: wc_sb[:, 16 + k:17 + k]
    db_c = lambda k: wc_sb[:, 32 + k:33 + k]

    def ln_rows(tc, save):
        v.tensor_copy(xhi[:], x_tc[:]).inc(ev_v)
        s.activation(sq[:], x_tc[:], AF.Square).inc(ev_s)
        t.wait(ev_v)
        t.wait(ev_s)
        for dmc in range(NDM):
            t.matmul(p_s1[:], C.ones_col[:], ts_slice(xhi, dmc, TC),
                     start=(dmc == 0), stop=(dmc == NDM - 1))
        for dmc in range(NDM):
            op = t.matmul(p_s2[:], C.ones_col[:], ts_slice(sq, dmc, TC),
                          start=(dmc == 0), stop=(dmc == NDM - 1))
        op.inc(mm)
        v.wait(mm)
        v.tensor_scalar_mul(mu_row[:], p_s1[:], 1.0 / D).inc(ev_v)
        s.wait(ev_v)
        s.activation(rstd_row[:], mu_row[:], AF.Square).inc(ev_s)
        v.wait(ev_s)
        v.scalar_tensor_tensor(t2_row[:], p_s2[:], 1.0 / D, rstd_row[:],
                               ALU.mult, ALU.subtract)
        v.tensor_scalar_add(rstd_row[:], t2_row[:], 1e-5).inc(ev_v)
        s.wait(ev_v)
        s.activation(rstd_row[:], rstd_row[:], AF.Sqrt).inc(ev_s)
        v.wait(ev_s)
        v.reciprocal(rstd_row[:], rstd_row[:]).inc(ev_v)
        if save:
            v.tensor_copy(lnmu[:, tc * TC:(tc + 1) * TC], mu_row[:])
            v.tensor_copy(lnrstd[:, tc * TC:(tc + 1) * TC], rstd_row[:]).inc(ev_v)
        t.wait(ev_v)
        t.matmul(p_mu[:], C.ones_row_f[:], mu_row[:], start=True, stop=True)
        t.matmul(p_rstd[:], C.ones_row_f[:], rstd_row[:], start=True, stop=True).inc(mm)

    slot = 0
    slot_mm = []
    sdst_n = [0, 0]
    for tc in range(NTC):
        y.wait(st)
        y.wait(ev_v)
        y.dma_start(out=x_tc[:], in_=xT[tc]).inc(ld, 16)
        v.wait(ld)
        s.wait(ld)
        ln_rows(tc, save=True)
        v.wait(mm)
        v.tensor_mul(rr_row[:], t2_row[:], rstd_row[:])
        v.tensor_mul(rr_row[:], rr_row[:], rstd_row[:])
        v.tensor_scalar_add(rr_row[:], rr_row[:], 1e-6).inc(ev_v)
        s.wait(ev_v)
        s.activation(rr_row[:], rr_row[:], AF.Sqrt).inc(ev_s)
        v.wait(ev_s)
        v.reciprocal(rr_row[:], rr_row[:])
        v.tensor_mul(rr_row[:], rr_row[:], rstd_row[:]).inc(ev_v)
        t.wait(ev_v)
        t.matmul(p_rstd[:], C.ones_row_f[:], rr_row[:], start=True, stop=True).inc(mm)
        v.wait(mm)
        for dmc in range(NDM):
            v.tensor_sub(t1f[:], ts_slice(x_tc, dmc, TC), p_mu[:])
            v.tensor_mul(t1f[:], t1f[:], p_rstd[:])
            op = v.tensor_scalar_mul(ts_slice(hbuf, dmc, TC), t1f[:], wpre_c(dmc))
        op.inc(ev_v)

        for k in range(NDM):
            par = slot % 2
            base = par * 3
            if slot >= 2:
                g.wait(mm, slot_mm[slot - 2])
            g.dma_start(out=ts_slice(scr, base + 0, D), in_=wa_w[k]).inc(wl[par], 16)
            g.dma_start(out=ts_slice(scr, base + 1, D), in_=wi_w[k]).inc(wl[par], 16)
            g.dma_start(out=ts_slice(scr, base + 2, D), in_=wv_w[k]).inc(wl[par], 16)
            t.wait(wl[par], wl[par].n)
            t.wait(ev_v)
            t.wait(ev_s)
            for dmc in range(NDM):
                t.matmul(pA[:], scr[:, (base + 0) * D + dmc * 128:(base + 0) * D + dmc * 128 + 128],
                         ts_slice(hbuf, dmc, TC), start=(dmc == 0), stop=(dmc == NDM - 1))
            for dmc in range(NDM):
                t.matmul(pB[:], scr[:, (base + 1) * D + dmc * 128:(base + 1) * D + dmc * 128 + 128],
                         ts_slice(hbuf, dmc, TC), start=(dmc == 0), stop=(dmc == NDM - 1))
            for dmc in range(NDM):
                op = t.matmul(pC[:], scr[:, (base + 2) * D + dmc * 128:(base + 2) * D + dmc * 128 + 128],
                              ts_slice(hbuf, dmc, TC), start=(dmc == 0), stop=(dmc == NDM - 1))
            op.inc(mm)
            slot_mm.append(mm.n)
            s.wait(mm)
            v.wait(mm)
            s.activation(af[:], pA[:], AF.Sigmoid, bias=db_c(k))
            s.activation(if_[:], pB[:], AF.Sigmoid).inc(ev_s)
            v.tensor_copy(vf[:], pC[:]).inc(ev_v)
            s.activation(t1f[:], af[:], AF.Square).inc(ev_s)
            v.wait(ev_s)
            v.tensor_scalar(t1f[:], t1f[:], -1.0, 1.0, ALU.mult, ALU.add)
            v.tensor_scalar_max(t1f[:], t1f[:], 1e-8).inc(ev_v)
            s.wait(ev_v)
            s.activation(t1f[:], t1f[:], AF.Sqrt).inc(ev_s)
            v.wait(ev_s)
            v.tensor_mul(if_[:], t1f[:], if_[:])
            v.tensor_mul(if_[:], if_[:], vf[:])
            v.tensor_scalar_max(t1f[:], af[:], 1e-10)
            for ch in range(TC // CH):
                v.tensor_tensor_scan(cdt[:, ch * CH:(ch + 1) * CH],
                                     t1f[:, ch * CH:(ch + 1) * CH],
                                     onesf[:, ch * CH:(ch + 1) * CH],
                                     1.0, ALU.mult, ALU.mult)
            v.tensor_scalar(cdt[:], cdt[:], 1e10, 1.0, ALU.mult, ALU.min)
            v.tensor_mul(cdt[:], cdt[:], if_[:])
            if tc == 0:
                v.tensor_tensor_scan(srf[:], af[:], cdt[:], 0.0, ALU.mult, ALU.add)
                v.tensor_tensor_scan(drf[:], af[:], onesf[:], 1.0, ALU.mult, ALU.mult)
            else:
                v.tensor_tensor_scan(srf[:], af[:], cdt[:], carry[:, k:k + 1],
                                     ALU.mult, ALU.add)
                v.tensor_tensor_scan(drf[:], af[:], onesf[:], dcarry[:, k:k + 1],
                                     ALU.mult, ALU.mult)
            v.tensor_copy(carry[:, k:k + 1], srf[:, TC - 1:TC])
            v.tensor_copy(dcarry[:, k:k + 1], drf[:, TC - 1:TC])
            v.wait(sdst[par], sdst_n[par])
            v.tensor_copy(sdl[par][:, 0:TC], srf[:])
            v.tensor_copy(sdl[par][:, TC:2 * TC], drf[:]).inc(ev_v)
            y.wait(ev_v)
            y.dma_start(out=s_dram[tc, k], in_=sdl[par][:, 0:TC]).inc(sdst[par], 16)
            y.dma_start(out=d_dram[tc, k], in_=sdl[par][:, TC:2 * TC]).inc(sdst[par], 16)
            sdst_n[par] = sdst[par].n
            slot += 1

    v.tensor_copy(agsb[:, 0:16], dcarry[:])
    v.tensor_copy(agsb[:, 16:32], carry[:]).inc(ev_v)
    g.wait(ev_v)
    g.dma_start(out=ag_in, in_=agsb[:]).inc(agst, 16)
    g.wait(agst, 16)
    g.collective_compute(
        "AllGather", ALU.bypass,
        replica_groups=[[0, 1, 2, 3], [4, 5, 6, 7]],
        ins=[ag_in], outs=[ag_out],
    ).inc(agc)
    g.wait(agc, 1)
    for r in range(4):
        g.dma_start(out=rtile[r][:], in_=ag_out[r * 128:(r + 1) * 128, :]).inc(agld, 16)
    v.wait(agld, 64)
    v.memset(st0[:], 0.0)
    for r in range(4):
        v.tensor_mul(t3[:], rtile[r][:, 0:16], st0[:])
        v.tensor_add(t3[:], t3[:], rtile[r][:, 16:32])
        v.tensor_scalar_mul(t3[:], t3[:], mk_sb[:, r:r + 1])
        v.tensor_scalar_mul(st0[:], st0[:], mk_sb[:, 4 + r:5 + r])
        op = v.tensor_add(st0[:], st0[:], t3[:])
    op.inc(ev_v)

    sd_slot = 0
    sd_evv = []
    wo_slot = 0
    wo_mm = []
    gu_slot = 0
    gu_evs = []
    dn_mm = 0
    for tc in range(NTC):
        y.wait(st)
        y.wait(ev_v)
        y.dma_start(out=x_tc[:], in_=xT[tc]).inc(ld, 16)
        v.wait(ld)
        t.wait(ev_v)
        t.matmul(p_mu[:], C.ones_row_f[:], lnmu[:, tc * TC:(tc + 1) * TC],
                 start=True, stop=True)
        t.matmul(p_rstd[:], C.ones_row_f[:], lnrstd[:, tc * TC:(tc + 1) * TC],
                 start=True, stop=True).inc(mm)
        for k in range(NDM):
            par = sd_slot % 2
            if sd_slot >= 2:
                y.wait(ev_v, sd_evv[sd_slot - 2])
            y.dma_start(out=sdl[par][:, 0:TC], in_=s_dram[tc, k]).inc(sdls[par], 16)
            y.dma_start(out=sdl[par][:, TC:2 * TC], in_=d_dram[tc, k]).inc(sdls[par], 16)
            v.wait(sdls[par], sdls[par].n)
            v.scalar_tensor_tensor(ts_slice(hbuf, k, TC), sdl[par][:, TC:2 * TC],
                                   st0[:, k:k + 1], sdl[par][:, 0:TC],
                                   ALU.mult, ALU.add).inc(ev_v)
            sd_evv.append(ev_v.n)
            sd_slot += 1
        for dmo in range(NDM):
            par = wo_slot % 2
            if wo_slot >= 2:
                g.wait(mm, wo_mm[wo_slot - 2])
            g.dma_start(out=wo[par][:], in_=wout_w[dmo]).inc(wol[par], 16)
            t.wait(wol[par], wol[par].n)
            if dmo == 0:
                t.wait(ev_v)
            for k in range(NDM):
                op = t.matmul(pA[:], wo[par][:, k * 128:(k + 1) * 128],
                              ts_slice(hbuf, k, TC), start=(k == 0), stop=(k == NDM - 1))
            op.inc(mm)
            wo_mm.append(mm.n)
            wo_slot += 1
            v.wait(mm)
            v.tensor_sub(t1f[:], ts_slice(x_tc, dmo, TC), p_mu[:])
            v.tensor_mul(t1f[:], t1f[:], p_rstd[:])
            v.tensor_add(ts_slice(zx, dmo, TC), t1f[:], pA[:]).inc(ev_v)
            t.wait(ev_v)
        s.wait(ev_v)
        s.activation(sq[:], zx[:], AF.Square).inc(ev_s)
        t.wait(ev_s)
        for dmc in range(NDM):
            op = t.matmul(p_s2[:], C.ones_col[:], ts_slice(sq, dmc, TC),
                          start=(dmc == 0), stop=(dmc == NDM - 1))
        op.inc(mm)
        v.wait(mm)
        v.tensor_scalar(rr_row[:], p_s2[:], 1.0 / D, 1e-6, ALU.mult, ALU.add).inc(ev_v)
        s.wait(ev_v)
        s.activation(rr_row[:], rr_row[:], AF.Sqrt).inc(ev_s)
        v.wait(ev_s)
        v.reciprocal(rr_row[:], rr_row[:]).inc(ev_v)
        t.wait(ev_v)
        t.matmul(p_rstd[:], C.ones_row_f[:], rr_row[:], start=True, stop=True).inc(mm)
        v.wait(mm)
        for dmc in range(NDM):
            v.tensor_mul(t1f[:], ts_slice(zx, dmc, TC), p_rstd[:])
            op = v.tensor_scalar_mul(ts_slice(hbuf, dmc, TC), t1f[:], wffn_c(dmc))
        op.inc(ev_v)
        t.wait(ev_v)
        for fg in range(8):
            g.wait(mm, dn_mm)
            for fjj in range(8):
                g.dma_start(out=ts_slice(scr, fjj, D),
                            in_=wdn_w[fg * 8 + fjj]).inc(dnl, 16)
            dn_ready = dnl.n
            for fjj in range(8):
                fj = fg * 8 + fjj
                par = gu_slot % 2
                if gu_slot >= 2:
                    g.wait(ev_s, gu_evs[gu_slot - 2])
                g.dma_start(out=wgu[par][:, 0:D], in_=wgu_w[fj]).inc(wl[par], 16)
                g.dma_start(out=wgu[par][:, D:2 * D], in_=wgu_w[64 + fj]).inc(wl[par], 16)
                t.wait(wl[par], wl[par].n)
                t.wait(ev_s)
                t.wait(ev_v)
                for dmc in range(NDM):
                    t.matmul(pA[:], wgu[par][:, dmc * 128:dmc * 128 + 128],
                             ts_slice(hbuf, dmc, TC), start=(dmc == 0), stop=(dmc == NDM - 1))
                for dmc in range(NDM):
                    op = t.matmul(pB[:], wgu[par][:, D + dmc * 128:D + dmc * 128 + 128],
                                  ts_slice(hbuf, dmc, TC), start=(dmc == 0), stop=(dmc == NDM - 1))
                op.inc(mm)
                s.wait(mm)
                v.wait(mm)
                s.activation(af[:], pA[:], AF.Silu).inc(ev_s)
                gu_evs.append(ev_s.n)
                v.tensor_copy(t1f[:], pB[:])
                v.wait(ev_s)
                v.tensor_mul(ts_slice(xhi, fjj, TC), af[:], t1f[:]).inc(ev_v)
                gu_slot += 1
            t.wait(dnl, dn_ready)
            t.wait(ev_v)
            for dmo in range(NDM):
                for fjj in range(8):
                    op = t.matmul(pC[:], scr[:, fjj * D + dmo * 128:fjj * D + dmo * 128 + 128],
                                  ts_slice(xhi, fjj, TC),
                                  start=(fjj == 0), stop=(fjj == 7))
                op.inc(mm)
                v.wait(mm)
                v.tensor_add(ts_slice(zx, dmo, TC), ts_slice(zx, dmo, TC), pC[:]).inc(ev_v)
                t.wait(ev_v)
            dn_mm = mm.n
        for dmo in range(NDM):
            op = v.tensor_add(ts_slice(zx, dmo, TC), ts_slice(x_tc, dmo, TC),
                              ts_slice(zx, dmo, TC))
        op.inc(ev_v)
        y.wait(ev_v)
        y.dma_start(out=yT[tc], in_=zx[:]).inc(st, 16)
    y.wait(st)
    y.wait(sdst[0], sdst[0].n)
    y.wait(sdst[1], sdst[1].n)


def prep_griffin_weights(inputs, core):
    wa = np.asarray(inputs["L0_attn_w_a"], dtype=np.float32)
    wi = np.asarray(inputs["L0_attn_w_i"], dtype=np.float32)
    wv = np.asarray(inputs["L0_attn_w_v"], dtype=np.float32)
    wout = np.asarray(inputs["L0_attn_out_proj_w"], dtype=np.float32)
    wgu = np.asarray(inputs["L0_attn_w_gate_up"], dtype=np.float32)
    wdn = np.asarray(inputs["L0_attn_w_down"], dtype=np.float32)
    wpre = np.asarray(inputs["L0_attn_pre_norm_w"], dtype=np.float32)
    wffn = np.asarray(inputs["L0_attn_ffn_norm_w"], dtype=np.float32)
    db = np.asarray(inputs["L0_attn_decay_bias"], dtype=np.float32)

    def k_slabs(w):
        return np.ascontiguousarray(
            w.reshape(NDM, 128, NDM, 128).transpose(2, 1, 0, 3).reshape(NDM, 128, NDM * 128))

    wgu_s = np.ascontiguousarray(
        wgu.reshape(NDM, 128, 128, 128).transpose(2, 1, 0, 3).reshape(128, 128, NDM * 128))
    wdn_s = np.ascontiguousarray(wdn.reshape(64, 128, D))
    wcols = np.ascontiguousarray(np.concatenate([
        wpre.reshape(NDM, 128).T, wffn.reshape(NDM, 128).T, db.reshape(NDM, 128).T,
    ], axis=1))
    r = core % 4
    alpha = np.zeros((128, 8), np.float32)
    for rr in range(4):
        a = 1.0 if rr < r else 0.0
        alpha[:, rr] = a
        alpha[:, 4 + rr] = 1.0 - a
    return {"wa_w": k_slabs(wa), "wi_w": k_slabs(wi), "wv_w": k_slabs(wv),
            "wout_w": k_slabs(wout), "wgu_w": wgu_s, "wdn_w": wdn_s,
            "wcols": wcols, "masks": alpha}


def build_griffin_program():
    nc = bass.Bass("TRN2", target_bir_lowering=False)
    S = Sched(nc)
    xT = nc.dram_tensor("xT", [NTC, 128, NDM * TC], F32, kind="ExternalInput").ap()
    wa_w = nc.dram_tensor("wa_w", [NDM, 128, NDM * 128], F32, kind="ExternalInput").ap()
    wi_w = nc.dram_tensor("wi_w", [NDM, 128, NDM * 128], F32, kind="ExternalInput").ap()
    wv_w = nc.dram_tensor("wv_w", [NDM, 128, NDM * 128], F32, kind="ExternalInput").ap()
    wout_w = nc.dram_tensor("wout_w", [NDM, 128, NDM * 128], F32, kind="ExternalInput").ap()
    wgu_w = nc.dram_tensor("wgu_w", [128, 128, NDM * 128], F32, kind="ExternalInput").ap()
    wdn_w = nc.dram_tensor("wdn_w", [64, 128, D], F32, kind="ExternalInput").ap()
    wcols = nc.dram_tensor("wcols", [128, 48], F32, kind="ExternalInput").ap()
    masks = nc.dram_tensor("masks", [128, 8], F32, kind="ExternalInput").ap()
    s_dram = nc.dram_tensor("s_scr", [NTC, NDM, 128, TC], BF16, kind="Internal").ap()
    d_dram = nc.dram_tensor("d_scr", [NTC, NDM, 128, TC], BF16, kind="Internal").ap()
    ag_in = nc.dram_tensor("ag_in", [128, 32], F32, kind="Internal").ap()
    ag_out = nc.dram_tensor("ag_out", [512, 32], F32, kind="Internal").ap()
    consts = declare_consts(nc)
    yT = nc.dram_tensor("yT", [NTC, 128, NDM * TC], F32, kind="ExternalOutput").ap()
    C = Common(S, consts)
    emit_griffin(S, C, xT, yT, wa_w, wi_w, wv_w, wout_w, wgu_w, wdn_w,
                 wcols, masks, s_dram=s_dram, d_dram=d_dram,
                 ag_in=ag_in, ag_out=ag_out)
    S.realize()
    return nc


def run_griffin(x_shards, inputs):
    nc = build_griffin_program()
    cn = host_consts()
    in_maps = []
    for c in range(8):
        w = prep_griffin_weights(inputs, c)
        in_maps.append({"xT": x_shards[c], **w, **cn})
    res = run_bass_kernel_spmd(nc, in_maps, core_ids=list(range(8)))
    return [res.results[c]["yT"] for c in range(8)]




def emit_l1ffn(S, C, xT, yT, wgT, a_w, bw_w, scale=0.5):
    g, y, t, s, v = S.g, S.y, S.t, S.s, S.v
    SCALING = 4.0

    x_tc = S.sbuf("f_x", (128, NDM * TC), F32)
    xhi = S.sbuf("f_xhi", (128, NDM * TC), BF16)
    sq = S.sbuf("f_sq", (128, NDM * TC), BF16)
    h_tc = S.sbuf("f_h", (128, NDM * TC), BF16)
    wg_sb = S.sbuf("f_wg", (128, NDM * 8), BF16)
    aw_sb = S.sbuf("f_aw", (128, 8 * NDM * 64), BF16)
    bw_sb = S.sbuf("f_bw", (64, 8 * NDM * 128), BF16)
    mu_row = S.sbuf("f_mu", (1, TC), F32)
    rstd_row = S.sbuf("f_rstd", (1, TC), F32)
    tmp = S.sbuf("f_tmp", (128, TC), F32)
    exrow = S.sbuf("f_ex", (128, 8), F32)
    prow = S.sbuf("f_p", (128, 8), F32)
    m1c = S.sbuf("f_m1", (128, 1), F32)
    m2c = S.sbuf("f_m2", (128, 1), F32)
    keepr = S.sbuf("f_keep", (128, 8), F32)
    crow = S.sbuf("f_c", (128, 8), BF16)
    cT = S.sbuf("f_cT", (8, TC), BF16)
    ce_bc = S.sbuf("f_cebc", (128, 8 * TC), F32)
    hsb = S.sbuf("f_hsb", (64, TC), BF16)
    hs_all = S.sbuf("f_hs", (64, 8 * TC), BF16)
    yacc = S.sbuf("f_yacc", (128, NDM * TC), F32)

    p_s1 = S.psum("fp_s1", (1, TC))
    p_s2 = S.psum("fp_s2", (1, TC))
    p_mu = S.psum("fp_mu", (128, TC))
    p_rstd = S.psum("fp_rstd", (128, TC))
    p_ct = S.psum("fp_ct", (8, TC), BF16)
    p_e = S.psum("fp_e", (128, TC))
    p_h = S.psum("fp_h", (64, TC))

    ld = S.sem("f_ld")
    wld = S.sem("f_wld")
    mm = S.sem("f_mm")
    ev_v = S.sem("f_ev_v")
    ev_s = S.sem("f_ev_s")
    st = S.sem("f_st")

    g.dma_start(out=wg_sb[:], in_=wgT[:]).inc(wld, 16)
    g.dma_start(out=aw_sb[:].rearrange("p (e f) -> p e f", e=8),
                in_=a_w.rearrange("e p f -> p e f")).inc(wld, 16)
    g.dma_start(out=bw_sb[:].rearrange("p (e f) -> p e f", e=8),
                in_=bw_w.rearrange("e p f -> p e f")).inc(wld, 16)

    for tc in range(NTC):
        y.wait(st)
        y.wait(ev_v)
        y.dma_start(out=x_tc[:], in_=xT[tc]).inc(ld, 16)
        v.wait(ld)
        s.wait(ld)

        emit_ln_stats(S, C, x_tc, xhi, sq, p_s1, p_s2, mu_row, rstd_row,
                      p_mu, p_rstd, mm, ev_v, ev_s)
        v.wait(mm)
        for dmc in range(NDM):
            v.tensor_sub(tmp[:], ts_slice(x_tc, dmc, TC), p_mu[:])
            op = v.tensor_mul(ts_slice(h_tc, dmc, TC), tmp[:], p_rstd[:])
        op.inc(ev_v)

        t.wait(ev_v)
        t.wait(wld, 48)
        for tt in range(TC // 128):
            for dmc in range(NDM):
                op = t.matmul(p_e[:, 0:8],
                              h_tc[:, dmc * TC + tt * 128: dmc * TC + tt * 128 + 128],
                              ts_slice(wg_sb, dmc, 8),
                              start=(dmc == 0), stop=(dmc == NDM - 1))
            op.inc(mm)
            v.wait(mm)
            s.wait(mm)
            v.reduce_max(m1c[:], p_e[:, 0:8], axis=AX, negate=True).inc(ev_v)
            s.wait(ev_v)
            s.activation(exrow[:], p_e[:, 0:8], AF.Exp, bias=m1c[:]).inc(ev_s)
            v.wait(ev_s)
            v.reduce_sum(m2c[:], exrow[:], axis=AX)
            v.reciprocal(m2c[:], m2c[:])
            v.tensor_scalar_mul(prow[:], exrow[:], m2c[:])
            v.reduce_max(m1c[:], prow[:], axis=AX)
            v.tensor_scalar(keepr[:], prow[:], m1c[:], None, ALU.is_ge)
            v.tensor_scalar(crow[:], keepr[:], m1c[:], float(scale) * SCALING,
                            ALU.mult, ALU.mult).inc(ev_v)
            t.wait(ev_v)
            t.wait(ev_s)
            t.transpose(p_ct[:, tt * 128:(tt + 1) * 128], crow[:], C.idn[:]).inc(mm)
        v.wait(mm)
        v.tensor_copy(cT[:], p_ct[:]).inc(ev_v)
        t.wait(ev_v)
        for e in range(8):
            t.matmul(p_e[:], C.sel_e(e), cT[:], start=True, stop=True).inc(mm)
            v.wait(mm)
            v.tensor_copy(ts_slice(ce_bc, e, TC), p_e[:]).inc(ev_v)
            t.wait(ev_v)

        for e in range(8):
            for dmc in range(NDM):
                op = t.matmul(p_h[:], aw_sb[:, (e * NDM + dmc) * 64:(e * NDM + dmc) * 64 + 64],
                              ts_slice(h_tc, dmc, TC),
                              start=(dmc == 0), stop=(dmc == NDM - 1))
            op.inc(mm)
            v.wait(mm)
            v.tensor_copy(hsb[:], p_h[:])
            v.tensor_mul(ts_slice(hs_all, e, TC), hsb[:], ce_bc[0:64, e * TC:(e + 1) * TC]) \
                .inc(ev_v)
            t.wait(ev_v)

        for dmo in range(NDM):
            for e in range(8):
                op = t.matmul(p_e[:], bw_sb[:, (e * NDM + dmo) * 128:(e * NDM + dmo) * 128 + 128],
                              ts_slice(hs_all, e, TC),
                              start=(e == 0), stop=(e == 7))
            op.inc(mm)
            v.wait(mm)
            v.tensor_add(ts_slice(yacc, dmo, TC), ts_slice(x_tc, dmo, TC), p_e[:]) \
                .inc(ev_v)
            t.wait(ev_v)
        y.wait(ev_v)
        y.dma_start(out=yT[tc], in_=yacc[:]).inc(st, 16)
    y.wait(st)


def prep_l1ffn_weights(inputs):
    A = np.asarray(inputs["L1_ffn_A"], dtype=np.float32)
    Bw = np.asarray(inputs["L1_ffn_Bw"], dtype=np.float32)
    Wg = np.asarray(inputs["L1_ffn_Wg"], dtype=np.float32)
    a_w = np.ascontiguousarray(
        A.transpose(0, 2, 1).reshape(8, NDM, 128, 64).transpose(0, 2, 1, 3)
        .reshape(8, 128, NDM * 64))
    bw_w = np.ascontiguousarray(Bw.transpose(0, 2, 1).reshape(8, 64, NDM * 128))
    wgT = np.ascontiguousarray(
        Wg.T.reshape(NDM, 128, 8).transpose(1, 0, 2).reshape(128, NDM * 8))
    return {"wgT": wgT, "a_w": a_w, "bw_w": bw_w}


def build_l1ffn_program():
    nc = bass.Bass("TRN2", target_bir_lowering=False)
    S = Sched(nc)
    xT = nc.dram_tensor("xT", [NTC, 128, NDM * TC], F32, kind="ExternalInput").ap()
    wgT = nc.dram_tensor("wgT", [128, NDM * 8], F32, kind="ExternalInput").ap()
    a_w = nc.dram_tensor("a_w", [8, 128, NDM * 64], F32, kind="ExternalInput").ap()
    bw_w = nc.dram_tensor("bw_w", [8, 64, NDM * 128], F32, kind="ExternalInput").ap()
    consts = declare_consts(nc)
    yT = nc.dram_tensor("yT", [NTC, 128, NDM * TC], F32, kind="ExternalOutput").ap()
    C = Common(S, consts)
    emit_l1ffn(S, C, xT, yT, wgT, a_w, bw_w, scale=0.5)
    S.realize()
    return nc


def run_l1ffn(x_shards, inputs, scale=0.5):
    nc = build_l1ffn_program()
    cn = host_consts()
    w = prep_l1ffn_weights(inputs)
    in_maps = [{"xT": x_shards[c], **w, **cn} for c in range(8)]
    res = run_bass_kernel_spmd(nc, in_maps, core_ids=list(range(8)))
    return [res.results[c]["yT"] for c in range(8)]

SEQ = 8192
NKT = SEQ // 128
NQC = SEQ // TC


def emit_l1attn(S, C, xT, yT, qkv_w, qkvb, woT_w, outb, slopes, t0_base, mmask,
                noff, selmask, hg_in, hg_all, qkv_scr, ag2_in, ag2_out):
    g, y, t, s, v = S.g, S.y, S.t, S.s, S.v
    SCALE = 1.0 / np.sqrt(128.0)

    x_tc = S.sbuf("t_x", (128, NDM * TC), F32)
    xhi = S.sbuf("t_xhi", (128, NDM * TC), BF16)
    sq = S.sbuf("t_sq", (128, NDM * TC), BF16)
    h_tc = S.sbuf("t_h", (128, NDM * TC), BF16)
    rhsb = xhi
    qt = S.sbuf("t_qt", (128, SEQ), BF16)
    kt_sb = S.sbuf("t_kt", (128, SEQ), BF16)
    vt_sb = S.sbuf("t_vt", (128, SEQ), BF16)
    vtok = S.sbuf("t_vtok", (128, SEQ), BF16)
    aout = h_tc
    rhs4list = [qt, kt_sb, vt_sb, vtok]
    qw = [S.sbuf(f"t_qw{i}", (128, NDM * 128), BF16) for i in range(2)]
    t0h = S.sbuf("t_t0h", (128, 5 * TC), F32)
    offtab = S.sbuf("t_offtab", (128, 64), F32)
    ssb = S.sbuf("t_ssb", (128, TC), F32)
    pt_sb = S.sbuf("t_pt", (128, TC), BF16)
    of32 = S.sbuf("t_of32", (128, TC), F32)
    t1f = S.sbuf("t_t1", (128, TC), F32)
    qkvb_sb = S.sbuf("t_qkvb", (128, 12), F32)
    outb_sb = S.sbuf("t_outb", (128, 16), F32)
    slop_sb = S.sbuf("t_slop", (1, 4), F32)
    t0b_sb = S.sbuf("t_t0b", (128, TC), F32)
    mmask_sb = S.sbuf("t_mm", (128, 4 * TC), F32)
    noff_sb = S.sbuf("t_noff", (1, 64), F32)
    selm_sb = S.sbuf("t_selm", (128, 4), F32)
    mu_row = S.sbuf("t_mur", (1, TC), F32)
    rstd_row = S.sbuf("t_rsr", (1, TC), F32)
    m_row = S.sbuf("t_mrow", (1, 1), F32)
    m2_row = S.sbuf("t_m2row", (1, 1), F32)
    sl_col = S.sbuf("t_slcol", (128, 1), F32)
    m_col = S.sbuf("t_mcol", (128, 1), F32)
    jcol = S.sbuf("t_jcol", (128, 1), F32)
    lrow = S.sbuf("t_lrow", (1, TC), F32)

    p_s1 = S.psum("tp_s1", (1, TC))
    p_s2 = S.psum("tp_s2", (1, TC))
    p_mu = S.psum("tp_mu", (128, TC))
    p_rstd = S.psum("tp_rstd", (128, TC))
    pA = S.psum("tp_A", (128, TC))
    pO = S.psum("tp_O", (128, TC))
    pL = S.psum("tp_L", (1, TC))
    pT = S.psum("tp_T", (128, 128), BF16)

    ld = S.sem("t_ld")
    cld = S.sem("t_cld")
    mm = S.sem("t_mm")
    ev_v = S.sem("t_ev_v")
    ev_s = S.sem("t_ev_s")
    st = S.sem("t_st")
    qwl = [S.sem(f"t_qwl{i}") for i in range(2)]
    rhl = S.sem("t_rhl")
    qkl = S.sem("t_qkl")
    hst = S.sem("t_hst")
    qst = S.sem("t_qst")
    ast = S.sem("t_ast")
    agc1 = S.sem("t_agc1")
    agc2 = S.sem("t_agc2")

    g.dma_start(out=qkvb_sb[:], in_=qkvb).inc(cld, 16)
    g.dma_start(out=outb_sb[:], in_=outb).inc(cld, 16)
    g.dma_start(out=slop_sb[:], in_=slopes).inc(cld, 16)
    g.dma_start(out=t0b_sb[:], in_=t0_base).inc(cld, 16)
    g.dma_start(out=mmask_sb[:].rearrange("p (j n) -> p j n", j=4),
                in_=mmask.rearrange("j p n -> p j n")).inc(cld, 16)
    g.dma_start(out=noff_sb[:], in_=noff).inc(cld, 16)
    g.dma_start(out=selm_sb[:], in_=selmask).inc(cld, 16)
    v.wait(cld, 112)
    s.wait(cld, 112)
    t.wait(cld, 112)
    v.tensor_scalar_mul(qkvb_sb[:, 0:4], qkvb_sb[:, 0:4], SCALE).inc(ev_v)

    for tc in range(NTC):
        y.wait(hst)
        y.wait(ev_v)
        y.dma_start(out=x_tc[:], in_=xT[tc]).inc(ld, 16)
        v.wait(ld)
        s.wait(ld)
        emit_ln_stats(S, C, x_tc, xhi, sq, p_s1, p_s2, mu_row, rstd_row,
                      p_mu, p_rstd, mm, ev_v, ev_s)
        v.wait(mm)
        for dmc in range(NDM):
            v.tensor_sub(t1f[:], ts_slice(x_tc, dmc, TC), p_mu[:])
            op = v.tensor_mul(ts_slice(h_tc, dmc, TC), t1f[:], p_rstd[:])
        op.inc(ev_v)
        y.wait(ev_v)
        y.dma_start(out=hg_in[:, tc * TC:(tc + 1) * TC].rearrange("(c p) n -> p c n", p=128),
                    in_=h_tc[:].rearrange("p (c n) -> p c n", c=NDM)).inc(hst, 16)
    g.wait(hst, hst.n)
    g.collective_compute(
        "AllGather", ALU.bypass,
        replica_groups=[[0, 1, 2, 3], [4, 5, 6, 7]],
        ins=[hg_in], outs=[hg_all],
    ).inc(agc1)
    g.wait(agc1, 1)

    qslot = 0
    qs_mm = []
    rhs_mm = 0
    y.wait(agc1, 1)
    for tb in range(SEQ // TC):
        rr_ = tb // 4
        tl_ = (tb % 4) * TC
        y.wait(mm, rhs_mm)
        y.dma_start(out=rhsb[:].rearrange("p (c n) -> p c n", c=NDM),
                    in_=hg_all[rr_ * D:(rr_ + 1) * D, tl_:tl_ + TC]
                    .rearrange("(c p) n -> p c n", p=128)).inc(rhl, 16)
        for oc in range(12):
            par = qslot % 2
            if qslot >= 2:
                g.wait(mm, qs_mm[qslot - 2])
            g.dma_start(out=qw[par][:], in_=qkv_w[oc]).inc(qwl[par], 16)
            t.wait(qwl[par], qwl[par].n)
            t.wait(rhl, rhl.n)
            t.wait(ev_s)
            for dmc in range(NDM):
                op = t.matmul(pA[:], qw[par][:, dmc * 128:dmc * 128 + 128],
                              ts_slice(rhsb, dmc, TC),
                              start=(dmc == 0), stop=(dmc == NDM - 1))
            op.inc(mm)
            qs_mm.append(mm.n)
            qslot += 1
            s.wait(mm)
            sc = SCALE if oc < 4 else 1.0
            s.activation(of32[:], pA[:], AF.Identity, scale=sc).inc(ev_s)
            v.wait(ev_s)
            v.tensor_scalar_add(pt_sb[:], of32[:], qkvb_sb[:, oc:oc + 1]).inc(ev_v)
            y.wait(ev_v)
            y.dma_start(out=qkv_scr[oc, :, tb * TC:(tb + 1) * TC], in_=pt_sb[:]) \
                .inc(qst, 16)
        rhs_mm = mm.n

    v.wait(hst, hst.n)
    for hh in range(4):
        y.wait(qst, qst.n)
        y.wait(ast)
        y.dma_start(out=qt[:], in_=qkv_scr[hh]).inc(qkl, 16)
        y.dma_start(out=kt_sb[:], in_=qkv_scr[4 + hh]).inc(qkl, 16)
        y.dma_start(out=vt_sb[:], in_=qkv_scr[8 + hh]).inc(qkl, 16)
        t.wait(qkl, qkl.n)
        for ktile in range(NKT):
            t.wait(ev_v)
            t.transpose(pT[:], vt_sb[:, ktile * 128:(ktile + 1) * 128], C.idn[:]).inc(mm)
            v.wait(mm)
            v.tensor_copy(vtok[:, ktile * 128:(ktile + 1) * 128], pT[:]).inc(ev_v)
        s.wait(qkl, qkl.n)
        s.wait(ev_v, ev_v.n)
        v.memset(m_row[:], -1e30)
        s.activation(sq[:], qt[:], AF.Square).inc(ev_s)
        t.wait(ev_s)
        for ch in range(NQC):
            t.wait(ev_v)
            t.matmul(p_s1[:], C.ones_col[:], ts_slice(sq, ch, TC),
                     start=True, stop=True).inc(mm)
            v.wait(mm)
            v.reduce_max(lrow[:, 0:1], p_s1[:], axis=AX)
            v.tensor_max(m_row[:], m_row[:], lrow[:, 0:1]).inc(ev_v)
        s.wait(ev_v)
        v.memset(m2_row[:], -1e30)
        s.activation(sq[:], kt_sb[:], AF.Square).inc(ev_s)
        t.wait(ev_s)
        for ch in range(NQC):
            t.wait(ev_v)
            t.matmul(p_s1[:], C.ones_col[:], ts_slice(sq, ch, TC),
                     start=True, stop=True).inc(mm)
            v.wait(mm)
            v.reduce_max(lrow[:, 0:1], p_s1[:], axis=AX)
            v.tensor_max(m2_row[:], m2_row[:], lrow[:, 0:1]).inc(ev_v)
        v.tensor_mul(m2_row[:], m2_row[:], m_row[:]).inc(ev_v)
        s.wait(ev_v)
        s.activation(m2_row[:], m2_row[:], AF.Sqrt).inc(ev_s)
        t.wait(ev_s)
        t.matmul(p_mu[:, 0:1], C.ones_row_f[:], m2_row[:], start=True, stop=True)
        t.matmul(p_rstd[:, 0:1], C.ones_row_f[:], slop_sb[:, hh:hh + 1],
                 start=True, stop=True).inc(mm)
        v.wait(mm)
        v.tensor_copy(m_col[:], p_mu[:, 0:1])
        v.tensor_copy(sl_col[:], p_rstd[:, 0:1])
        v.tensor_scalar_mul(t0h[:, 4 * TC:5 * TC], t0b_sb[:], sl_col[:])
        for j in range(4):
            v.tensor_scalar(jcol[:], sl_col[:], float(128 * j), None, ALU.mult)
            v.tensor_sub(jcol[:], jcol[:], m_col[:])
            v.tensor_add(ts_slice(t0h, j, TC), t0h[:, 4 * TC:5 * TC],
                         ts_slice(mmask_sb, j, TC))
            v.tensor_scalar(ts_slice(t0h, j, TC), ts_slice(t0h, j, TC), jcol[:],
                            None, ALU.add)
        v.engine_nop() if False else None
        op = v.tensor_copy(jcol[:], jcol[:])
        op.inc(ev_v)
        t.wait(ev_v)
        t.matmul(p_mu[:, 0:64], C.ones_row_f[:], noff_sb[:], start=True, stop=True) \
            .inc(mm)
        v.wait(mm)
        v.tensor_scalar_mul(offtab[:], p_mu[:, 0:64], sl_col[:])
        v.tensor_scalar(offtab[:], offtab[:], m_col[:], None, ALU.subtract).inc(ev_v)
        t.wait(ev_v)

        for qc in range(NQC):
            nkt_q = 4 * qc + 4
            for ktile in range(nkt_q):
                off = ktile - 4 * qc
                t.wait(ev_v)
                t.matmul(pA[:], kt_sb[:, ktile * 128:(ktile + 1) * 128],
                         qt[:, qc * TC:(qc + 1) * TC], start=True, stop=True).inc(mm)
                v.wait(mm)
                if off >= 0:
                    v.tensor_add(ssb[:], ts_slice(t0h, off, TC), pA[:]).inc(ev_v)
                else:
                    v.scalar_tensor_tensor(ssb[:], t0h[:, 4 * TC:5 * TC],
                                           offtab[:, -off:1 - off], pA[:],
                                           ALU.add, ALU.add).inc(ev_v)
                s.wait(ev_v)
                s.activation(pt_sb[:], ssb[:], AF.Exp).inc(ev_s)
                t.wait(ev_s)
                t.matmul(pO[:], vtok[:, ktile * 128:(ktile + 1) * 128], pt_sb[:],
                         start=(ktile == 0), stop=(ktile == nkt_q - 1))
                op = t.matmul(pL[:], C.ones_col[:], pt_sb[:],
                              start=(ktile == 0), stop=(ktile == nkt_q - 1))
                if ktile == nkt_q - 1:
                    op.inc(mm)
            v.wait(mm)
            v.tensor_copy(lrow[:], pL[:])
            v.reciprocal(lrow[:], lrow[:]).inc(ev_v)
            t.wait(ev_v)
            t.matmul(p_rstd[:], C.ones_row_f[:], lrow[:], start=True, stop=True).inc(mm)
            s.wait(mm)
            s.copy(of32[:], pO[:]).inc(ev_s)
            v.wait(ev_s)
            v.wait(ast)
            v.tensor_mul(aout[:, qc * TC:(qc + 1) * TC], of32[:], p_rstd[:]).inc(ev_v)
            t.wait(ev_v)
        y.wait(ev_v)
        y.dma_start(out=ag2_in[hh * 128:(hh + 1) * 128, :], in_=aout[:]).inc(ast, 16)

    g.wait(ast, ast.n)
    g.collective_compute(
        "AllGather", ALU.bypass,
        replica_groups=[[0, 1, 2, 3], [4, 5, 6, 7]],
        ins=[ag2_in], outs=[ag2_out],
    ).inc(agc2)
    g.wait(agc2, 1)

    woslot = 0
    wo_mm = []
    sel_mm = 0
    y.wait(agc2, 1)
    for tc in range(NTC):
        y.wait(st)
        y.wait(ev_v)
        y.dma_start(out=x_tc[:], in_=xT[tc]).inc(ld, 16)
        y.wait(mm, sel_mm)
        for gb in range(4):
            y.dma_start(out=rhs4list[gb][:].rearrange("p (c n) -> p c n", c=NDM),
                        in_=ag2_out[:, gb * TOK + tc * TC: gb * TOK + (tc + 1) * TC]
                        .rearrange("(c p) n -> p c n", p=128)).inc(rhl, 16)
        v.wait(rhl, rhl.n)
        v.wait(ld)
        for dmc in range(NDM):
            v.tensor_scalar_mul(ts_slice(rhsb, dmc, TC),
                                ts_slice(rhs4list[0], dmc, TC), selm_sb[:, 0:1])
        for gb in range(1, 4):
            for dmc in range(NDM):
                v.tensor_scalar(t1f[:], ts_slice(rhs4list[gb], dmc, TC),
                                selm_sb[:, gb:gb + 1], None, ALU.mult)
                v.tensor_add(ts_slice(rhsb, dmc, TC), ts_slice(rhsb, dmc, TC), t1f[:])
        v.tensor_copy(t1f[:], t1f[:]).inc(ev_v)
        t.wait(ev_v)
        for oc in range(NDM):
            par = woslot % 2
            if woslot >= 2:
                g.wait(mm, wo_mm[woslot - 2])
            g.dma_start(out=qw[par][:], in_=woT_w[oc]).inc(qwl[par], 16)
            t.wait(qwl[par], qwl[par].n)
            t.wait(ev_s)
            for dvc in range(NDM):
                op = t.matmul(pA[:], qw[par][:, dvc * 128:dvc * 128 + 128],
                              ts_slice(rhsb, dvc, TC),
                              start=(dvc == 0), stop=(dvc == NDM - 1))
            op.inc(mm)
            wo_mm.append(mm.n)
            woslot += 1
            s.wait(mm)
            s.copy(of32[:], pA[:]).inc(ev_s)
            v.wait(ev_s)
            v.tensor_scalar_add(t1f[:], of32[:], outb_sb[:, oc:oc + 1])
            v.tensor_add(ts_slice(x_tc, oc, TC), ts_slice(x_tc, oc, TC), t1f[:]) \
                .inc(ev_v)
        sel_mm = mm.n
        y.wait(ev_v)
        y.dma_start(out=yT[tc], in_=x_tc[:]).inc(st, 16)
    y.wait(st)


def prep_l1attn_weights(inputs, core):
    qkv = np.asarray(inputs["L1_attn_qkv_w"], dtype=np.float32)
    qkvb = np.asarray(inputs["L1_attn_qkv_b"], dtype=np.float32)
    outw = np.asarray(inputs["L1_attn_out_w"], dtype=np.float32)
    outb = np.asarray(inputs["L1_attn_out_b"], dtype=np.float32)
    slopes = np.asarray(inputs["L1_attn_alibi_slopes"], dtype=np.float32)
    gme = core % 4
    heads = [4 * gme + i for i in range(4)]
    rows = []
    bias = []
    for sec in range(3):
        for h in heads:
            rows.append(qkv[sec * 2048 + h * 128: sec * 2048 + (h + 1) * 128, :])
            bias.append(qkvb[sec * 2048 + h * 128: sec * 2048 + (h + 1) * 128])
    qkv_slice = np.concatenate(rows, axis=0)
    qkvT = qkv_slice.T
    qkv_s = np.ascontiguousarray(
        qkvT.reshape(NDM, 128, 12, 128).transpose(2, 1, 0, 3).reshape(12, 128, NDM * 128))
    qkvb_cols = np.ascontiguousarray(np.stack(bias, axis=1))
    woT = outw.T
    wo_s = np.ascontiguousarray(
        woT.reshape(NDM, 128, NDM, 128).transpose(2, 1, 0, 3).reshape(NDM, 128, NDM * 128))
    outb_cols = np.ascontiguousarray(outb.reshape(NDM, 128).T)
    slop = np.ascontiguousarray(slopes[heads][None, :])
    kk = np.arange(128, dtype=np.float32)[:, None]
    qq = np.arange(TC, dtype=np.float32)[None, :]
    t0 = kk - qq
    mmask = np.zeros((4, 128, TC), np.float32)
    for j in range(4):
        mmask[j][(kk + 128 * j) > qq] = -1e9
    noff = (-128.0 * np.arange(64, dtype=np.float32))[None, :]
    selm = np.zeros((128, 4), np.float32)
    selm[:, gme] = 1.0
    return {"qkv_w": qkv_s, "qkvb": qkvb_cols, "woT_w": wo_s, "outb": outb_cols,
            "slopes": slop, "t0_base": np.ascontiguousarray(t0),
            "mmask": mmask, "noff": noff, "selmask": selm}


def build_l1attn_program():
    nc = bass.Bass("TRN2", target_bir_lowering=False)
    S = Sched(nc)
    xT = nc.dram_tensor("xT", [NTC, 128, NDM * TC], F32, kind="ExternalInput").ap()
    qkv_w = nc.dram_tensor("qkv_w", [12, 128, NDM * 128], F32, kind="ExternalInput").ap()
    qkvb = nc.dram_tensor("qkvb", [128, 12], F32, kind="ExternalInput").ap()
    woT_w = nc.dram_tensor("woT_w", [NDM, 128, NDM * 128], F32, kind="ExternalInput").ap()
    outb = nc.dram_tensor("outb", [128, 16], F32, kind="ExternalInput").ap()
    slopes = nc.dram_tensor("slopes", [1, 4], F32, kind="ExternalInput").ap()
    t0_base = nc.dram_tensor("t0_base", [128, TC], F32, kind="ExternalInput").ap()
    mmask = nc.dram_tensor("mmask", [4, 128, TC], F32, kind="ExternalInput").ap()
    noff = nc.dram_tensor("noff", [1, 64], F32, kind="ExternalInput").ap()
    selmask = nc.dram_tensor("selmask", [128, 4], F32, kind="ExternalInput").ap()
    hg_in = nc.dram_tensor("hg_in", [D, TOK], BF16, kind="Internal").ap()
    hg_all = nc.dram_tensor("hg_all", [4 * D, TOK], BF16, kind="Internal").ap()
    qkv_scr = nc.dram_tensor("qkv_scr", [12, 128, SEQ], BF16, kind="Internal").ap()
    ag2_in = nc.dram_tensor("ag2_in", [512, SEQ], BF16, kind="Internal").ap()
    ag2_out = nc.dram_tensor("ag2_out", [D, SEQ], BF16, kind="Internal").ap()
    consts = declare_consts(nc)
    yT = nc.dram_tensor("yT", [NTC, 128, NDM * TC], F32, kind="ExternalOutput").ap()
    C = Common(S, consts)
    emit_l1attn(S, C, xT, yT, qkv_w, qkvb, woT_w, outb, slopes, t0_base, mmask,
                noff, selmask, hg_in, hg_all, qkv_scr, ag2_in, ag2_out)
    S.realize()
    return nc


def run_l1attn(x_shards, inputs):
    nc = build_l1attn_program()
    cn = host_consts()
    in_maps = []
    for c in range(8):
        w = prep_l1attn_weights(inputs, c)
        in_maps.append({"xT": x_shards[c], **w, **cn})
    res = run_bass_kernel_spmd(nc, in_maps, core_ids=list(range(8)))
    return [res.results[c]["yT"] for c in range(8)]





def build_moe_program():
    nc = bass.Bass("TRN2", target_bir_lowering=False)
    S = Sched(nc)
    xT = nc.dram_tensor("xT", [NTC, 128, NDM * TC], F32, kind="ExternalInput").ap()
    rw = nc.dram_tensor("rw", [128, NDM * 8], F32, kind="ExternalInput").ap()
    gate_w = nc.dram_tensor("gate_w", [8, 8, 128, NDM * 128], F32, kind="ExternalInput").ap()
    up_w = nc.dram_tensor("up_w", [8, 8, 128, NDM * 128], F32, kind="ExternalInput").ap()
    down_w = nc.dram_tensor("down_w", [8, 1024, D], F32, kind="ExternalInput").ap()
    consts = declare_consts(nc)
    yT = nc.dram_tensor("yT", [NTC, 128, NDM * TC], F32, kind="ExternalOutput").ap()
    C = Common(S, consts)
    emit_moe(S, C, xT, yT, rw, gate_w, up_w, down_w, scale=0.5)
    S.realize()
    return nc



def shard_xT(x):
    xf = x.reshape(2 * 8192, 2048).T
    out = []
    for c in range(8):
        sh = xf[:, c * TOK:(c + 1) * TOK]
        t = sh.reshape(NDM, 128, NTC, TC).transpose(2, 1, 0, 3).reshape(NTC, 128, NDM * TC)
        out.append(np.ascontiguousarray(t))
    return out


def unshard_xT(shards):
    cols = []
    for c in range(8):
        t = shards[c].reshape(NTC, 128, NDM, TC).transpose(2, 1, 0, 3).reshape(D, TOK)
        cols.append(t)
    xf = np.concatenate(cols, axis=1)
    return np.ascontiguousarray(xf.T).reshape(2, 8192, 2048)


def prep_moe_weights(inputs):
    gate = np.asarray(inputs["L0_ffn_gate_proj"], dtype=np.float32)
    up = np.asarray(inputs["L0_ffn_up_proj"], dtype=np.float32)
    down = np.asarray(inputs["L0_ffn_down_proj"], dtype=np.float32)
    rw = np.asarray(inputs["L0_ffn_router_w"], dtype=np.float32)
    def arr_gu(w):
        return np.ascontiguousarray(
            w.reshape(8, NDM, 128, 8, 128).transpose(0, 3, 2, 1, 4)
            .reshape(8, 8, 128, NDM * 128))
    rw2 = np.ascontiguousarray(rw.reshape(NDM, 128, 8).transpose(1, 0, 2).reshape(128, NDM * 8))
    return {"gate_w": arr_gu(gate), "up_w": arr_gu(up),
            "down_w": np.ascontiguousarray(down), "rw": rw2}


def run_moe(x_shards, inputs, scale=0.5):
    nc = build_moe_program()
    cn = host_consts()
    w = prep_moe_weights(inputs)
    in_maps = []
    for c in range(8):
        in_maps.append({"xT": x_shards[c], **w, **cn})
    res = run_bass_kernel_spmd(nc, in_maps, core_ids=list(range(8)))
    return [res.results[c]["yT"] for c in range(8)]


def kernel(**inputs):
    x = np.asarray(inputs["L0_attn_x"], dtype=np.float32)
    shards = shard_xT(x)
    shards = run_moe(shards, inputs, scale=0.5)
    shards = run_griffin(shards, inputs)
    shards = run_moe(shards, inputs, scale=0.5)
    shards = run_l1ffn(shards, inputs, scale=0.5)
    shards = run_l1attn(shards, inputs)
    shards = run_l1ffn(shards, inputs, scale=0.5)
    return unshard_xT(shards)


if __name__ == "__main__":
    pass
